# version 03
# baseline (speedup 1.0000x reference)
import os
import numpy as np
import concourse.bass as bass
import concourse.mybir as mybir
from concourse.bass_utils import run_bass_kernel_spmd
from contextlib import ExitStack

F32 = mybir.dt.float32
BF16 = mybir.dt.bfloat16
AF = mybir.ActivationFunctionType
ALU = mybir.AluOpType
AX = mybir.AxisListType.X

KCH = int(os.environ.get("KCH", "0")) or None
KPH = int(os.environ.get("KPH", "99"))
T, D = 2048, 2048
TC = 512
NCH = T // TC
NTG = TC // 128
NDC = D // 128
NKC = D // 128
NST = 16
DTR = 128
NEXP = 32
FEXP = 768
NFC = FEXP // 128
FSH = 1536
EPS = 1e-5
W1 = 516
TH = 1024
NSB = TH // 512

ENGS = ("sync", "tensor", "vector", "scalar", "gpsimd")


class Sem:
    def __init__(self, nc, name):
        self.h = nc.alloc_semaphore(name=name)
        self.v = 0
        self.name = name


def build_nc():
    nc = bass.Bass("TRN2", target_bir_lowering=False)
    nc.detect_race_conditions = False

    x_h = nc.dram_tensor("x", [T, D], F32, kind="ExternalInput").ap()
    cw_h = nc.dram_tensor("conv_w", [D, 4], F32, kind="ExternalInput").ap()
    cb_h = nc.dram_tensor("conv_b", [D], F32, kind="ExternalInput").ap()
    xpw_h = nc.dram_tensor("xproj_w", [160, D], F32, kind="ExternalInput").ap()
    dtw_h = nc.dram_tensor("dtproj_w", [D, DTR], F32, kind="ExternalInput").ap()
    dtb_h = nc.dram_tensor("dtproj_b", [D], F32, kind="ExternalInput").ap()
    al_h = nc.dram_tensor("A_log", [D, NST], F32, kind="ExternalInput").ap()
    dp_h = nc.dram_tensor("D_param", [D], F32, kind="ExternalInput").ap()
    gw_h = nc.dram_tensor("gate_w", [D, NEXP], F32, kind="ExternalInput").ap()
    wg_h = nc.dram_tensor("wg", [NEXP, D, FEXP], F32, kind="ExternalInput").ap()
    wu_h = nc.dram_tensor("wu", [NEXP, D, FEXP], F32, kind="ExternalInput").ap()
    wd_h = nc.dram_tensor("wd", [NEXP, FEXP, D], F32, kind="ExternalInput").ap()
    sg_h = nc.dram_tensor("sg", [D, FSH], F32, kind="ExternalInput").ap()
    su_h = nc.dram_tensor("su", [D, FSH], F32, kind="ExternalInput").ap()
    sd_h = nc.dram_tensor("sd", [FSH, D], F32, kind="ExternalInput").ap()
    iden_h = nc.dram_tensor("iden", [128, 128], F32, kind="ExternalInput").ap()
    out_h = nc.dram_tensor("out", [T, D], F32, kind="ExternalOutput").ap()
    bc_d = nc.dram_tensor("bc_d", [32, T], F32, kind="Internal").ap()
    comb_d = nc.dram_tensor("comb_d", [NEXP, T], BF16, kind="Internal").ap()
    x2_d = nc.dram_tensor("x2_d", [T, D], F32, kind="Internal").ap()
    zb_d = nc.dram_tensor("zb_d", [NKC, 128, T], BF16, kind="Internal").ap()

    ctxP = ExitStack()
    ctxM = ExitStack()
    sbtP = lambda name, shape, dt=F32: ctxP.enter_context(
        nc.sbuf_tensor("sb_" + name, shape, dt))
    sbtM = lambda name, shape, dt=F32: ctxM.enter_context(
        nc.sbuf_tensor("sb_" + name, shape, dt))
    psb = lambda name, shape: ctxP.enter_context(nc.psum_tensor(name, shape, F32))

    iden = sbtP("iden", [128, 128])
    xin = sbtP("xin", [128, 2048])
    stats = sbtP("stats", [128, 16])
    cvt = sbtP("cvt", [128, 128])

    xpwT = sbtM("xpwT", [128, NKC * 160])
    dtpwT = sbtM("dtpwT", [128, D])
    aneg = sbtM("aneg", [128, NDC * NST])
    cw = sbtM("cw", [128, NDC * 4])
    cb = sbtM("cbv", [128, NDC])
    db = sbtM("dbv", [128, NDC])
    dpv = sbtM("dpv", [128, NDC])
    gwT = sbtM("gwT", [128, NKC * 32])
    P1 = sbtM("P1", [128, NDC * W1])
    P2 = sbtM("P2", [128, NDC * 512])
    P3 = sbtM("P3", [128, NDC * 512])
    P4 = sbtM("P4", [128, NDC * 512])
    P5 = sbtM("P5", [128, 2048])
    a2 = sbtM("a2", [128, 1024])
    bbuf = sbtM("bbuf", [128, 512])
    h2 = sbtM("h2", [128, 1024])
    ytb = sbtM("ytb", [128, 1024])
    state = sbtM("state", [128, 256])
    brep = sbtM("brep", [128, 2048])
    dtT = sbtM("dtT", [128, 512])
    zt2 = sbtM("zt2", [128, 1024], BF16)
    bcT = sbtM("bcT", [32, 512])
    halo = sbtM("halo", [128, NDC * 3])
    spl = sbtM("spl", [128, 1024])
    scrT = spl[0:32, 0:512]
    cmbT = spl[0:32, 512:1024]
    cmbTb = zt2[0:32, 0:512]
    scr_tm = a2[:, 0:128]
    m8t = a2[:, 256:288]
    selb = a2[:, 288:320]
    cmbb = a2[:, 320:352]

    ps = [psb(f"ps{i}", [128, 512]) for i in range(6)]
    pt = [psb(f"pt{i}", [128, 128]) for i in range(2)]

    s_ld = Sem(nc, "s_ld")
    s_st = Sem(nc, "s_st")
    s_pe = Sem(nc, "s_pe")
    s_dve = Sem(nc, "s_dve")
    s_act = Sem(nc, "s_act")
    s_pool = Sem(nc, "s_pool")
    ALLSEMS = (s_ld, s_st, s_pe, s_dve, s_act, s_pool)

    for sm in ALLSEMS:
        nc.gpsimd.sem_clear(sm.h)
    nc.all_engine_barrier()

    EM = {e: [] for e in ENGS}
    WAITS = []

    def op(eng, fn, incs=()):
        def thunk(fn=fn, incs=tuple(incs)):
            ins = fn()
            for s, k in incs:
                ins = ins.then_inc(s.h, k)
        EM[eng].append(thunk)
        for s, k in incs:
            s.v += k

    def wt(eng, sem, val):
        if val <= 0:
            return
        WAITS.append((sem, val))
        EM[eng].append(lambda sem=sem, val=val, eng=eng:
                       getattr(nc, eng).wait_ge(sem.h, val))

    E = {e: getattr(nc, e) for e in ENGS}

    bank_free = {}

    def pe_claim(bank):
        for sem, val in bank_free.get(bank, ()):
            wt("tensor", sem, val)
        bank_free[bank] = []

    def bank_consumed(bank, sem):
        bank_free.setdefault(bank, []).append((sem, sem.v))

    bfc = lambda apx: apx.bitcast(BF16)

    def bhalf(tile_ap, c0, width):
        return bfc(tile_ap)[:, 2 * c0 + 1: 2 * (c0 + width): 2]

    def wait_list(eng, lst):
        for sem, val in lst:
            wt(eng, sem, val)

    op("vector", lambda: E["vector"].memset(state[:], 0.0))
    for dc in range(NDC):
        last = dc == NDC - 1
        op("vector", lambda dc=dc: E["vector"].memset(
            P1[:, dc * W1: dc * W1 + 4], 0.0),
            incs=([(s_dve, 1)] if last else []))

    op("sync", lambda: E["sync"].dma_start(out=iden[:], in_=iden_h),
       incs=[(s_ld, 16)])
    op("sync", lambda: E["sync"].dma_start(
        out=cw[:].rearrange("p (c k) -> p c k", k=4),
        in_=cw_h.rearrange("(c p) k -> p c k", p=128)), incs=[(s_ld, 16)])
    op("sync", lambda: E["sync"].dma_start(
        out=aneg[:].rearrange("p (c n) -> p c n", n=NST),
        in_=al_h.rearrange("(c p) n -> p c n", p=128)), incs=[(s_ld, 16)])
    op("sync", lambda: E["sync"].dma_start(
        out=gwT[:].rearrange("p (c e) -> p c e", e=NEXP),
        in_=gw_h.rearrange("(c p) e -> p c e", p=128)), incs=[(s_ld, 16)])
    v0 = s_ld.v

    wt("scalar", s_ld, v0)
    op("scalar", lambda: E["scalar"].activation(
        out=aneg[:], in_=aneg[:], func=AF.Exp), incs=[(s_act, 1)])
    wt("vector", s_act, s_act.v)
    op("vector", lambda: E["vector"].tensor_scalar(
        out=aneg[:], in0=aneg[:], scalar1=-1.0, scalar2=None, op0=ALU.mult),
        incs=[(s_dve, 1)])

    for vec_h, dst in ((cb_h, cb), (dtb_h, db), (dp_h, dpv)):
        wt("sync", s_pe, s_pe.v)
        op("sync", lambda vec_h=vec_h: E["sync"].dma_start(
            out=bcT[0:16, 0:128], in_=vec_h.rearrange("(c p) -> c p", p=128)),
            incs=[(s_ld, 16)])
        wt("tensor", s_ld, s_ld.v)
        pe_claim("pt0")
        op("tensor", lambda: E["tensor"].transpose(
            pt[0][:, 0:16], bcT[0:16, 0:128], iden[0:16, 0:16]),
            incs=[(s_pe, 1)])
        wt("vector", s_pe, s_pe.v)
        op("vector", lambda dst=dst: E["vector"].tensor_copy(
            out=dst[:], in_=pt[0][:, 0:16]), incs=[(s_dve, 1)])
        bank_consumed("pt0", s_dve)

    op("sync", lambda: E["sync"].dma_start(out=xin[:], in_=xpw_h[0:128, :]),
       incs=[(s_ld, 16)])
    v = s_ld.v
    for kc in range(NKC):
        b = f"pt{kc % 2}"
        wt("tensor", s_ld, v)
        pe_claim(b)
        op("tensor", lambda kc=kc: E["tensor"].transpose(
            pt[kc % 2][:], xin[:, kc * 128:(kc + 1) * 128], iden[:]),
            incs=[(s_pe, 1)])
        wt("vector", s_pe, s_pe.v)
        op("vector", lambda kc=kc: E["vector"].tensor_copy(
            out=xpwT[:, kc * 160: kc * 160 + 128], in_=pt[kc % 2][:]),
            incs=[(s_dve, 1)])
        bank_consumed(b, s_dve)
    wt("sync", s_pe, s_pe.v)
    op("sync", lambda: E["sync"].dma_start(
        out=xin[0:32, :], in_=xpw_h[128:160, :]), incs=[(s_ld, 16)])
    v = s_ld.v
    for kc in range(NKC):
        b = f"pt{kc % 2}"
        wt("tensor", s_ld, v)
        pe_claim(b)
        op("tensor", lambda kc=kc: E["tensor"].transpose(
            pt[kc % 2][:, 0:32], xin[0:32, kc * 128:(kc + 1) * 128],
            iden[0:32, 0:32]), incs=[(s_pe, 1)])
        wt("vector", s_pe, s_pe.v)
        op("vector", lambda kc=kc: E["vector"].tensor_copy(
            out=xpwT[:, kc * 160 + 128: (kc + 1) * 160],
            in_=pt[kc % 2][:, 0:32]), incs=[(s_dve, 1)])
        bank_consumed(b, s_dve)

    for kc in range(NKC):
        half = (kc % 2) * 1024
        wt("sync", s_pe, s_pe.v)
        op("sync", lambda kc=kc, half=half: E["sync"].dma_start(
            out=xin[:, half:half + 128],
            in_=dtw_h[kc * 128:(kc + 1) * 128, :]), incs=[(s_ld, 16)])
        b = f"pt{kc % 2}"
        wt("tensor", s_ld, s_ld.v)
        pe_claim(b)
        op("tensor", lambda kc=kc, half=half: E["tensor"].transpose(
            pt[kc % 2][:], xin[:, half:half + 128], iden[:]), incs=[(s_pe, 1)])
        wt("vector", s_pe, s_pe.v)
        op("vector", lambda kc=kc: E["vector"].tensor_copy(
            out=dtpwT[:, kc * 128:(kc + 1) * 128], in_=pt[kc % 2][:]),
            incs=[(s_dve, 1)])
        bank_consumed(b, s_dve)

    free_xin = [(s_pe, s_pe.v)]
    free_a = [[], []]
    free_h = [[], []]
    free_yt = [[], []]
    free_brep = [[], []]
    free_spl = []
    free_dtT = []
    free_zt2 = [[], []]
    free_bcT = [(s_dve, s_dve.v)]

    v_sqred = [0]
    v_x2_spill = [0] * (NCH * NTG)
    v_comb_store = 0
    v_zb_spill = 0
    prev_end = []

    for c in range(KCH or NCH):
        if c > 0:
            for eng in ENGS:
                for sem, val in prev_end:
                    wt(eng, sem, val)

        if KPH < 1:
            continue
        for tg in range(NTG):
            wait_list("sync", free_xin)
            rows = c * TC + tg * 128
            op("sync", lambda rows=rows: E["sync"].dma_start(
                out=xin[:], in_=x_h[rows:rows + 128, :]), incs=[(s_ld, 16)])
            v_x = s_ld.v
            wt("scalar", s_ld, v_x)
            wt("scalar", s_dve, v_sqred[0])
            op("scalar", lambda: E["scalar"].activation(
                out=P5[:, 0:2048], in_=xin[:], func=AF.Square),
                incs=[(s_act, 1)])
            v_sq = s_act.v
            wt("vector", s_ld, v_x)
            op("vector", lambda: E["vector"].reduce_sum(
                stats[:, 0:1], xin[:], axis=AX), incs=[(s_dve, 1)])
            wt("vector", s_dve, s_dve.v)
            op("vector", lambda: E["vector"].tensor_scalar(
                out=stats[:, 0:1], in0=stats[:, 0:1], scalar1=1.0 / D,
                scalar2=None, op0=ALU.mult))
            wt("vector", s_act, v_sq)
            op("vector", lambda: E["vector"].reduce_sum(
                stats[:, 4:5], P5[:, 0:2048], axis=AX), incs=[(s_dve, 1)])
            v_sqred[0] = s_dve.v
            wt("vector", s_dve, s_dve.v)
            op("vector", lambda: E["vector"].tensor_scalar(
                out=stats[:, 4:5], in0=stats[:, 4:5], scalar1=1.0 / D,
                scalar2=None, op0=ALU.mult))
            op("vector", lambda: E["vector"].tensor_tensor(
                out=stats[:, 8:9], in0=stats[:, 0:1], in1=stats[:, 0:1],
                op=ALU.mult), incs=[(s_dve, 1)])
            wt("vector", s_dve, s_dve.v)
            op("vector", lambda: E["vector"].tensor_tensor(
                out=stats[:, 4:5], in0=stats[:, 4:5], in1=stats[:, 8:9],
                op=ALU.subtract), incs=[(s_dve, 1)])
            wt("scalar", s_dve, s_dve.v)
            op("scalar", lambda: E["scalar"].activation(
                out=stats[:, 8:9], in_=stats[:, 4:5], func=AF.Ln, bias=np.float32(EPS)))
            op("scalar", lambda: E["scalar"].activation(
                out=stats[:, 8:9], in_=stats[:, 8:9], func=AF.Exp, scale=np.float32(-0.5)),
                incs=[(s_act, 1)])
            wt("vector", s_act, s_act.v)
            op("vector", lambda: E["vector"].tensor_scalar(
                out=xin[:], in0=xin[:], scalar1=stats[:, 0:1],
                scalar2=stats[:, 8:9], op0=ALU.subtract, op1=ALU.mult),
                incs=[(s_dve, 1)])
            v_ln = s_dve.v
            for dc in range(NDC):
                b = f"pt{dc % 2}"
                wt("tensor", s_dve, v_ln)
                pe_claim(b)
                op("tensor", lambda dc=dc: E["tensor"].transpose(
                    pt[dc % 2][:], xin[:, dc * 128:(dc + 1) * 128], iden[:]),
                    incs=[(s_pe, 1)])
                wt("vector", s_pe, s_pe.v)
                op("vector", lambda dc=dc, tg=tg: E["vector"].tensor_copy(
                    out=P1[:, dc * W1 + 4 + tg * 128:
                            dc * W1 + 4 + (tg + 1) * 128],
                    in_=pt[dc % 2][:]), incs=[(s_dve, 1)])
                bank_consumed(b, s_dve)
            free_xin = [(s_pe, s_pe.v)]

        if c > 0:
            for dc in range(NDC):
                op("vector", lambda dc=dc: E["vector"].tensor_copy(
                    out=P1[:, dc * W1 + 1: dc * W1 + 4],
                    in_=halo[:, dc * 3:(dc + 1) * 3]),
                    incs=([(s_dve, 1)] if dc == NDC - 1 else []))

        if KPH < 2:
            continue
        v_conv_act = 0
        for dc in range(NDC):
            base = dc * W1
            wait_list("vector", free_spl)
            op("vector", lambda base=base, dc=dc: E["vector"].tensor_scalar(
                out=spl[:, 0:512], in0=P1[:, base + 1: base + 513],
                scalar1=cw[:, dc * 4: dc * 4 + 1], scalar2=None, op0=ALU.mult))
            for k in range(1, 4):
                op("vector", lambda base=base, dc=dc, k=k:
                   E["vector"].scalar_tensor_tensor(
                    out=spl[:, 0:512], in0=P1[:, base + 1 + k: base + 513 + k],
                    scalar=cw[:, dc * 4 + k: dc * 4 + k + 1],
                    in1=spl[:, 0:512], op0=ALU.mult, op1=ALU.add),
                    incs=([(s_dve, 1)] if k == 3 else []))
            wt("scalar", s_dve, s_dve.v)
            op("scalar", lambda dc=dc: E["scalar"].activation(
                out=P2[:, dc * 512:(dc + 1) * 512], in_=spl[:, 0:512],
                func=AF.Silu, bias=cb[:, dc:dc + 1]), incs=[(s_act, 1)])
            free_spl = [(s_act, s_act.v)]
            v_conv_act = s_act.v
            op("vector", lambda dc=dc, base=base: E["vector"].tensor_scalar(
                out=P4[:, dc * 512:(dc + 1) * 512],
                in0=P1[:, base + 4: base + 516],
                scalar1=dpv[:, dc:dc + 1], scalar2=None, op0=ALU.mult))
            op("vector", lambda dc=dc, base=base: E["vector"].tensor_copy(
                out=halo[:, dc * 3:(dc + 1) * 3],
                in_=P1[:, base + 513: base + 516]),
                incs=([(s_dve, 1)] if dc == NDC - 1 else []))

        if KPH < 3:
            continue
        pe_claim("ps0")
        pe_claim("ps1")
        wt("tensor", s_act, v_conv_act)
        for kc in range(NKC):
            op("tensor", lambda kc=kc: E["tensor"].matmul(
                ps[0][:], xpwT[:, kc * 160: kc * 160 + 128],
                P2[:, kc * 512:(kc + 1) * 512],
                start=(kc == 0), stop=(kc == NKC - 1), skip_group_check=True),
                incs=([(s_pe, 1)] if kc == NKC - 1 else []))
            op("tensor", lambda kc=kc: E["tensor"].matmul(
                ps[1][0:32, :], xpwT[:, kc * 160 + 128: (kc + 1) * 160],
                P2[:, kc * 512:(kc + 1) * 512],
                start=(kc == 0), stop=(kc == NKC - 1), skip_group_check=True),
                incs=([(s_pe, 1)] if kc == NKC - 1 else []))
        v_xproj = s_pe.v
        wt("vector", s_pe, v_xproj)
        wait_list("vector", free_dtT)
        op("vector", lambda: E["vector"].tensor_copy(out=dtT[:], in_=ps[0][:]),
           incs=[(s_dve, 1)])
        bank_consumed("ps0", s_dve)
        v_dtT = s_dve.v
        wait_list("vector", free_bcT)
        op("vector", lambda: E["vector"].tensor_copy(
            out=bcT[:], in_=ps[1][0:32, :]), incs=[(s_dve, 1)])
        bank_consumed("ps1", s_dve)
        v_bcT = s_dve.v

        wt("sync", s_dve, v_bcT)
        op("sync", lambda c=c: E["sync"].dma_start(
            out=bc_d[:, c * TC:(c + 1) * TC], in_=bcT[:]), incs=[(s_st, 16)])
        v_bc_store = s_st.v
        free_bcT = [(s_st, v_bc_store)]

        if KPH < 4:
            continue
        v_dt_ready = 0
        for dc in range(NDC):
            b = f"ps{2 + dc % 2}"
            wt("tensor", s_dve, v_dtT)
            pe_claim(b)
            op("tensor", lambda dc=dc: E["tensor"].matmul(
                ps[2 + dc % 2][:], dtpwT[:, dc * 128:(dc + 1) * 128], dtT[:],
                start=True, stop=True), incs=[(s_pe, 1)])
            v_mm = s_pe.v
            psd = ps[2 + dc % 2]
            wt("scalar", s_pe, v_mm)
            wait_list("scalar", free_spl)
            op("scalar", lambda psd=psd, dc=dc: E["scalar"].activation(
                out=spl[:, 0:512], in_=psd[:], func=AF.Relu,
                bias=db[:, dc:dc + 1]))
            op("scalar", lambda psd=psd, dc=dc: E["scalar"].activation(
                out=spl[:, 512:1024], in_=psd[:], func=AF.Abs,
                bias=db[:, dc:dc + 1]), incs=[(s_act, 1)])
            bank_consumed(b, s_act)
            op("scalar", lambda: E["scalar"].activation(
                out=spl[:, 512:1024], in_=spl[:, 512:1024], func=AF.Exp,
                scale=np.float32(-1.0)))
            op("scalar", lambda: E["scalar"].activation(
                out=spl[:, 512:1024], in_=spl[:, 512:1024], func=AF.Ln,
                bias=np.float32(1.0)), incs=[(s_act, 1)])
            wt("vector", s_act, s_act.v)
            op("vector", lambda dc=dc: E["vector"].tensor_add(
                P3[:, dc * 512:(dc + 1) * 512], spl[:, 0:512],
                spl[:, 512:1024]), incs=[(s_dve, 1)])
            free_spl = [(s_dve, s_dve.v)]
            v_dt_ready = s_dve.v

        for dc in range(NDC):
            op("vector", lambda dc=dc: E["vector"].tensor_mul(
                P2[:, dc * 512:(dc + 1) * 512],
                P3[:, dc * 512:(dc + 1) * 512],
                P2[:, dc * 512:(dc + 1) * 512]),
                incs=([(s_dve, 1)] if dc == NDC - 1 else []))
        v_dx = s_dve.v

        if KPH < 5:
            continue
        it = 0
        v_scan_act = 0
        for n in range(NST):
            sl = n % 2
            wait_list("sync", free_brep[sl])
            wt("sync", s_st, v_bc_store)
            op("sync", lambda n=n, sl=sl, c=c: E["sync"].dma_start(
                out=brep[:, sl * 1024: sl * 1024 + 512],
                in_=bc_d[n:n + 1, c * TC:(c + 1) * TC].partition_broadcast(128)),
                incs=[(s_ld, 16)])
            op("sync", lambda n=n, sl=sl, c=c: E["sync"].dma_start(
                out=brep[:, sl * 1024 + 512: (sl + 1) * 1024],
                in_=bc_d[16 + n:17 + n, c * TC:(c + 1) * TC].partition_broadcast(128)),
                incs=[(s_ld, 16)])
            v_brep = s_ld.v
            free_brep[sl] = []
            last_b = last_c = 0
            for dc in range(NDC):
                asl = it % 2
                hsl = it % 2
                ysl = it % 2
                wait_list("scalar", free_a[asl])
                if it == 0:
                    wt("scalar", s_dve, v_dt_ready)
                op("scalar", lambda dc=dc, n=n, asl=asl:
                   E["scalar"].activation(
                    out=a2[:, asl * 512:(asl + 1) * 512],
                    in_=P3[:, dc * 512:(dc + 1) * 512], func=AF.Exp,
                    scale=aneg[:, dc * NST + n: dc * NST + n + 1]),
                    incs=[(s_act, 1)])
                free_a[asl] = []
                v_a = s_act.v
                v_scan_act = v_a
                if dc == 0:
                    wt("vector", s_ld, v_brep)
                op("vector", lambda dc=dc, sl=sl: E["vector"].tensor_mul(
                    bbuf[:], P2[:, dc * 512:(dc + 1) * 512],
                    brep[:, sl * 1024: sl * 1024 + 512]))
                wt("vector", s_act, v_a)
                wait_list("vector", free_h[hsl])
                op("vector", lambda dc=dc, n=n, asl=asl, hsl=hsl:
                   E["vector"].tensor_tensor_scan(
                    out=h2[:, hsl * 512:(hsl + 1) * 512],
                    data0=a2[:, asl * 512:(asl + 1) * 512], data1=bbuf[:],
                    initial=state[:, dc * NST + n: dc * NST + n + 1],
                    op0=ALU.mult, op1=ALU.add), incs=[(s_dve, 1)])
                free_a[asl] = [(s_dve, s_dve.v)]
                free_h[hsl] = []
                v_scan = s_dve.v
                last_b = s_dve.v
                wt("gpsimd", s_dve, v_scan)
                wait_list("gpsimd", free_yt[ysl])
                op("gpsimd", lambda sl=sl, hsl=hsl, ysl=ysl:
                   E["gpsimd"].tensor_mul(
                    ytb[:, ysl * 512:(ysl + 1) * 512],
                    h2[:, hsl * 512:(hsl + 1) * 512],
                    brep[:, sl * 1024 + 512: (sl + 1) * 1024]))
                free_yt[ysl] = []
                op("gpsimd", lambda dc=dc, n=n, hsl=hsl:
                   E["gpsimd"].tensor_copy(
                    out=state[:, dc * NST + n: dc * NST + n + 1],
                    in_=h2[:, hsl * 512 + 511: hsl * 512 + 512]),
                    incs=[(s_pool, 1)])
                free_h[hsl] = [(s_pool, s_pool.v)]
                last_c = s_pool.v
                if dc % 4 == 3:
                    op("gpsimd", lambda dc=dc, ysl=ysl: E["gpsimd"].tensor_add(
                        P4[:, dc * 512:(dc + 1) * 512],
                        P4[:, dc * 512:(dc + 1) * 512],
                        ytb[:, ysl * 512:(ysl + 1) * 512]), incs=[(s_pool, 1)])
                    free_yt[ysl] = [(s_pool, s_pool.v)]
                    last_c = s_pool.v
                else:
                    wt("vector", s_pool, s_pool.v)
                    op("vector", lambda dc=dc, ysl=ysl: E["vector"].tensor_add(
                        P4[:, dc * 512:(dc + 1) * 512],
                        P4[:, dc * 512:(dc + 1) * 512],
                        ytb[:, ysl * 512:(ysl + 1) * 512]), incs=[(s_dve, 1)])
                    free_yt[ysl] = [(s_dve, s_dve.v)]
                    last_b = s_dve.v
                it += 1
            free_brep[sl] = [(s_dve, last_b), (s_pool, last_c)]
        v_y_done_v = s_dve.v
        v_y_done_p = s_pool.v

        if KPH < 6:
            continue
        v_x2_list = [0] * NTG
        for tg in range(NTG):
            wait_list("sync", free_xin)
            rows = c * TC + tg * 128
            op("sync", lambda rows=rows: E["sync"].dma_start(
                out=xin[:], in_=x_h[rows:rows + 128, :]), incs=[(s_ld, 16)])
            v_x = s_ld.v
            for dc in range(NDC):
                b = f"pt{dc % 2}"
                wt("tensor", s_dve, v_y_done_v)
                if tg == 0 and dc == 0:
                    wt("tensor", s_pool, v_y_done_p)
                pe_claim(b)
                op("tensor", lambda dc=dc, tg=tg: E["tensor"].transpose(
                    pt[dc % 2][:],
                    P4[:, dc * 512 + tg * 128: dc * 512 + (tg + 1) * 128],
                    iden[:]), incs=[(s_pe, 1)])
                wt("vector", s_pe, s_pe.v)
                if dc == 0:
                    wt("vector", s_ld, v_x)
                    if tg == 0:
                        wt("vector", s_act, v_scan_act)
                op("vector", lambda dc=dc, tg=tg: E["vector"].tensor_add(
                    P3[:, tg * 2048 + dc * 128: tg * 2048 + (dc + 1) * 128],
                    pt[dc % 2][:], xin[:, dc * 128:(dc + 1) * 128]),
                    incs=[(s_dve, 1)])
                bank_consumed(b, s_dve)
            free_xin = [(s_dve, s_dve.v)]
            v_x2_list[tg] = s_dve.v
            wt("sync", s_dve, v_x2_list[tg])
            op("sync", lambda tg=tg, c=c: E["sync"].dma_start(
                out=x2_d[c * TC + tg * 128: c * TC + (tg + 1) * 128, :],
                in_=P3[:, tg * 2048:(tg + 1) * 2048]), incs=[(s_st, 16)])
            v_x2_spill[c * NTG + tg] = s_st.v
        v_yT_done = s_pe.v

        v_z = 0
        v_ln2_act = 0
        for tg in range(NTG):
            wt("scalar", s_dve, v_x2_list[tg])
            wt("scalar", s_dve, v_sqred[0])
            op("scalar", lambda tg=tg: E["scalar"].activation(
                out=P5[:, 0:2048], in_=P3[:, tg * 2048:(tg + 1) * 2048],
                func=AF.Square), incs=[(s_act, 1)])
            v_sq2 = s_act.v
            op("vector", lambda tg=tg: E["vector"].reduce_sum(
                stats[:, 0:1], P3[:, tg * 2048:(tg + 1) * 2048], axis=AX),
                incs=[(s_dve, 1)])
            wt("vector", s_dve, s_dve.v)
            op("vector", lambda: E["vector"].tensor_scalar(
                out=stats[:, 0:1], in0=stats[:, 0:1], scalar1=1.0 / D,
                scalar2=None, op0=ALU.mult))
            wt("vector", s_act, v_sq2)
            op("vector", lambda: E["vector"].reduce_sum(
                stats[:, 4:5], P5[:, 0:2048], axis=AX), incs=[(s_dve, 1)])
            v_sqred[0] = s_dve.v
            wt("vector", s_dve, s_dve.v)
            op("vector", lambda: E["vector"].tensor_scalar(
                out=stats[:, 4:5], in0=stats[:, 4:5], scalar1=1.0 / D,
                scalar2=None, op0=ALU.mult))
            op("vector", lambda: E["vector"].tensor_tensor(
                out=stats[:, 8:9], in0=stats[:, 0:1], in1=stats[:, 0:1],
                op=ALU.mult), incs=[(s_dve, 1)])
            wt("vector", s_dve, s_dve.v)
            op("vector", lambda: E["vector"].tensor_tensor(
                out=stats[:, 4:5], in0=stats[:, 4:5], in1=stats[:, 8:9],
                op=ALU.subtract), incs=[(s_dve, 1)])
            wt("scalar", s_dve, s_dve.v)
            op("scalar", lambda: E["scalar"].activation(
                out=stats[:, 8:9], in_=stats[:, 4:5], func=AF.Ln, bias=np.float32(EPS)))
            op("scalar", lambda: E["scalar"].activation(
                out=stats[:, 8:9], in_=stats[:, 8:9], func=AF.Exp, scale=np.float32(-0.5)),
                incs=[(s_act, 1)])
            v_ln2_act = s_act.v
            wt("vector", s_act, s_act.v)
            if tg == 0:
                wt("vector", s_pe, v_yT_done)
            op("vector", lambda tg=tg: E["vector"].tensor_scalar(
                out=P4[:, tg * 2048:(tg + 1) * 2048],
                in0=P3[:, tg * 2048:(tg + 1) * 2048],
                scalar1=stats[:, 0:1], scalar2=stats[:, 8:9],
                op0=ALU.subtract, op1=ALU.mult), incs=[(s_dve, 1)])
            v_z = s_dve.v

        if KPH < 7:
            continue
        v_spill_last = 0
        for tg in range(NTG):
            for dc in range(NDC):
                b = f"pt{dc % 2}"
                wt("tensor", s_dve, v_z)
                pe_claim(b)
                op("tensor", lambda tg=tg, dc=dc: E["tensor"].transpose(
                    pt[dc % 2][:],
                    P4[:, tg * 2048 + dc * 128: tg * 2048 + (dc + 1) * 128],
                    iden[:]), incs=[(s_pe, 1)])
                wt("vector", s_pe, s_pe.v)
                if tg == 0 and dc == 0:
                    wt("vector", s_st, v_x2_spill[c * NTG + NTG - 1])
                    wt("vector", s_act, v_ln2_act)
                op("vector", lambda tg=tg, dc=dc: E["vector"].tensor_copy(
                    out=P3[:, dc * 512 + tg * 128: dc * 512 + (tg + 1) * 128],
                    in_=pt[dc % 2][:]), incs=[(s_dve, 1)])
                bank_consumed(b, s_dve)
        v_zf = s_dve.v
        for dc in range(NDC):
            zsl = dc % 2
            wait_list("vector", free_zt2[zsl])
            op("vector", lambda dc=dc, zsl=zsl: E["vector"].tensor_copy(
                out=zt2[:, zsl * 512:(zsl + 1) * 512],
                in_=P3[:, dc * 512:(dc + 1) * 512]), incs=[(s_dve, 1)])
            free_zt2[zsl] = []
            wt("sync", s_dve, s_dve.v)
            op("sync", lambda dc=dc, zsl=zsl, c=c: E["sync"].dma_start(
                out=zb_d[dc, :, c * TC:(c + 1) * TC],
                in_=zt2[:, zsl * 512:(zsl + 1) * 512]), incs=[(s_st, 16)])
            free_zt2[zsl] = [(s_st, s_st.v)]
        v_zb_spill = s_st.v

        pe_claim("ps1")
        wt("tensor", s_dve, v_zf)
        for kc in range(NKC):
            op("tensor", lambda kc=kc: E["tensor"].matmul(
                ps[1][0:32, :], gwT[:, kc * 32:(kc + 1) * 32],
                P3[:, kc * 512:(kc + 1) * 512],
                start=(kc == 0), stop=(kc == NKC - 1)),
                incs=([(s_pe, 1)] if kc == NKC - 1 else []))
        v_gate = s_pe.v
        wt("scalar", s_pe, v_gate)
        op("scalar", lambda: E["scalar"].activation(
            out=scrT, in_=ps[1][0:32, :], func=AF.Sigmoid),
            incs=[(s_act, 1)])
        bank_consumed("ps1", s_act)
        v_scr = s_act.v
        for tg in range(NTG):
            b = f"pt{tg % 2}"
            wt("tensor", s_act, v_scr)
            pe_claim(b)
            op("tensor", lambda tg=tg: E["tensor"].transpose(
                pt[tg % 2][:, 0:32], scrT[:, tg * 128:(tg + 1) * 128],
                iden[0:32, 0:32]), incs=[(s_pe, 1)])
            wt("vector", s_pe, s_pe.v)
            op("vector", lambda tg=tg: E["vector"].tensor_copy(
                out=scr_tm[:, 0:32],
                in_=pt[tg % 2][:, 0:32]), incs=[(s_dve, 1)])
            bank_consumed(b, s_dve)
            wt("vector", s_dve, s_dve.v)
            op("vector", lambda tg=tg: E["vector"].max(
                out=m8t[:, 0:8],
                in_=scr_tm[:, 0:32]), incs=[(s_dve, 1)])
            wt("vector", s_dve, s_dve.v)
            op("vector", lambda tg=tg: E["vector"].tensor_scalar(
                out=selb[:], in0=scr_tm[:, 0:32],
                scalar1=m8t[:, 5:6],
                scalar2=None, op0=ALU.is_ge), incs=[(s_dve, 1)])
            wt("vector", s_dve, s_dve.v)
            op("vector", lambda tg=tg: E["vector"].tensor_mul(
                selb[:], selb[:], scr_tm[:, 0:32]),
                incs=[(s_dve, 1)])
            wt("vector", s_dve, s_dve.v)
            op("vector", lambda tg=tg: E["vector"].reduce_sum(
                stats[:, 8:9], selb[:], axis=AX), incs=[(s_dve, 1)])
            wt("vector", s_dve, s_dve.v)
            op("vector", lambda tg=tg: E["vector"].tensor_scalar(
                out=stats[:, 8:9], in0=stats[:, 8:9],
                scalar1=1e-20, scalar2=None, op0=ALU.add), incs=[(s_dve, 1)])
            wt("vector", s_dve, s_dve.v)
            op("vector", lambda tg=tg: E["vector"].reciprocal(
                out=stats[:, 12:13], in_=stats[:, 8:9]),
                incs=[(s_dve, 1)])
            wt("vector", s_dve, s_dve.v)
            op("vector", lambda tg=tg: E["vector"].tensor_scalar(
                out=cmbb[:], in0=selb[:], scalar1=stats[:, 12:13],
                scalar2=None, op0=ALU.mult), incs=[(s_dve, 1)])
            b = f"pt{tg % 2}"
            wt("tensor", s_dve, s_dve.v)
            pe_claim(b)
            op("tensor", lambda tg=tg: E["tensor"].transpose(
                pt[tg % 2][0:32, :], cmbb[:], iden[:]), incs=[(s_pe, 1)])
            wt("vector", s_pe, s_pe.v)
            op("vector", lambda tg=tg: E["vector"].tensor_copy(
                out=cmbT[:, tg * 128:(tg + 1) * 128],
                in_=pt[tg % 2][0:32, 0:128]), incs=[(s_dve, 1)])
            bank_consumed(b, s_dve)
        v_comb = s_dve.v
        wt("vector", s_dve, v_comb)
        wait_list("vector", free_zt2[0])
        op("vector", lambda: E["vector"].tensor_copy(out=cmbTb, in_=cmbT),
           incs=[(s_dve, 1)])
        free_zt2[0] = []
        wt("sync", s_dve, s_dve.v)
        op("sync", lambda c=c: E["sync"].dma_start(
            out=comb_d[:, c * TC:(c + 1) * TC], in_=cmbTb), incs=[(s_st, 16)])
        free_zt2[0] = [(s_st, s_st.v)]
        v_comb_store = s_st.v

        prev_end = [(s_dve, s_dve.v), (s_act, s_act.v), (s_pe, s_pe.v),
                    (s_pool, s_pool.v), (s_st, s_st.v), (s_ld, s_ld.v)]

    ctxM.close()
    ctxR = ExitStack()
    ctxF = ExitStack()
    sbtF = lambda name, shape, dt=F32: ctxF.enter_context(
        nc.sbuf_tensor("sb_" + name, shape, dt))
    routv = ctxR.enter_context(nc.sbuf_tensor("sb_routv", [128, NDC * 2048], BF16))
    zbh = sbtF("zbh", [128, NKC * 1024], BF16)
    actv = [sbtF(f"act{i}", [128, 2 * NFC * 1024], BF16) for i in range(2)]
    gus = sbtF("gus", [128, 2048])
    dns = sbtF("dns", [128, 2 * 12 * 256])
    wrep = sbtF("wrep", [128, 2 * 2048], BF16)
    sG = sbtF("sG", [128, 1024])
    uW = sbtF("uW", [128, 1024])

    v_F_end = [0, 0, 0, 0]
    if KPH >= 8:
        free_zbh = []
        free_gus = [[], []]
        free_dns = [[], []]
        free_wrep = [[], []]
        free_act = [[], []]
        free_sG = []
        free_uW = []
        gu_seq = [0]

        def gu_dma(e, fc, hf, mat):
            slot = gu_seq[0] % 2
            gu_seq[0] += 1
            wait_list("sync", free_gus[slot])
            if e < NEXP:
                src_h = wg_h if mat == 0 else wu_h
                src = src_h[e, :, fc * 128:(fc + 1) * 128].rearrange(
                    "(c p) f -> p c f", p=128)[:, hf * 8:(hf + 1) * 8, :]
            else:
                src_h = sg_h if mat == 0 else su_h
                s = e - NEXP
                src = src_h[:, s * FEXP + fc * 128: s * FEXP + (fc + 1) * 128
                            ].rearrange("(c p) f -> p c f", p=128)[
                                :, hf * 8:(hf + 1) * 8, :]
            col = slot * 1024
            dst = gus[:, col:col + 1024].rearrange("p (c f) -> p c f", f=128)
            op("sync", lambda src=src, dst=dst: E["sync"].dma_start(
                out=dst, in_=src), incs=[(s_ld, 16)])
            free_gus[slot] = []
            return slot, s_ld.v

        def dn_dma(dslot, idx, e, fkc, dg):
            if e < NEXP:
                src = wd_h[e, fkc * 128:(fkc + 1) * 128,
                           dg * 256:(dg + 1) * 256]
            else:
                s = e - NEXP
                src = sd_h[s * FEXP + fkc * 128: s * FEXP + (fkc + 1) * 128,
                           dg * 256:(dg + 1) * 256]
            col = dslot * 12 * 256 + idx * 256
            op("sync", lambda src=src, col=col: E["sync"].dma_start(
                out=dns[:, col:col + 256], in_=src), incs=[(s_ld, 16)])
            return s_ld.v

        first_gu = True
        for h in range(2):
            wait_list("sync", free_zbh)
            wt("sync", s_st, v_zb_spill)
            for kc in range(NKC):
                op("sync", lambda kc=kc, h=h: E["sync"].dma_start(
                    out=zbh[:, kc * 1024:(kc + 1) * 1024],
                    in_=zb_d[kc, :, h * TH:(h + 1) * TH]), incs=[(s_ld, 16)])
            v_zbh = s_ld.v
            free_zbh = []
            v_zbh_lastmm = 0

            for grp in range(NEXP // 2 + 1):
                es = [2 * grp, 2 * grp + 1]
                aslot = grp % 2
                wsl = grp % 2
                v_wrep = [0, 0]
                for ei, e in enumerate(es):
                    if e < NEXP:
                        if ei == 0:
                            wait_list("sync", free_wrep[wsl])
                            wt("sync", s_st, v_comb_store)
                        op("sync", lambda e=e, ei=ei, h=h, wsl=wsl:
                           E["sync"].dma_start(
                            out=wrep[:, wsl * 2048 + ei * 1024:
                                     wsl * 2048 + (ei + 1) * 1024],
                            in_=comb_d[e:e + 1, h * TH:(h + 1) * TH
                                       ].partition_broadcast(128)),
                            incs=[(s_ld, 16)])
                        v_wrep[ei] = s_ld.v
                        if ei == 1:
                            free_wrep[wsl] = []
                for ei, e in enumerate(es):
                    for fc in range(NFC):
                        pe_claim("ps0")
                        pe_claim("ps1")
                        pe_claim("ps2")
                        pe_claim("ps3")
                        for mat in (0, 1):
                            pst0 = ps[mat * 2]
                            pst1 = ps[mat * 2 + 1]
                            for hf in (0, 1):
                                slot, vld = gu_dma(e, fc, hf, mat)
                                wt("tensor", s_ld, vld)
                                if first_gu:
                                    wt("tensor", s_ld, v_zbh)
                                    first_gu = False
                                elif hf == 0 and fc == 0 and mat == 0 and ei == 0 and grp == 0:
                                    wt("tensor", s_ld, v_zbh)
                                for kc in range(hf * 8, hf * 8 + 8):
                                    col = slot * 1024 + (kc % 8) * 128
                                    lhsT = bhalf(gus[:], col, 128)
                                    for sub in range(NSB):
                                        pstt = pst0 if sub == 0 else pst1
                                        op("tensor",
                                           lambda pstt=pstt, lhsT=lhsT, kc=kc, sub=sub:
                                           E["tensor"].matmul(
                                            pstt[:], lhsT,
                                            zbh[:, kc * 1024 + sub * 512:
                                                kc * 1024 + (sub + 1) * 512],
                                            start=(kc == 0), stop=(kc == NKC - 1),
                                            skip_group_check=True),
                                            incs=([(s_pe, 1)]
                                                  if (kc % 8 == 7 and sub == NSB - 1)
                                                  else []))
                                free_gus[slot] = [(s_pe, s_pe.v)]
                        v_gu = s_pe.v
                        v_zbh_lastmm = s_pe.v
                        wt("scalar", s_pe, v_gu)
                        wait_list("scalar", free_sG)
                        op("scalar", lambda: E["scalar"].activation(
                            out=sG[:, 0:512], in_=ps[0][:], func=AF.Silu))
                        op("scalar", lambda: E["scalar"].activation(
                            out=sG[:, 512:1024], in_=ps[1][:], func=AF.Silu),
                            incs=[(s_act, 1)])
                        bank_consumed("ps0", s_act)
                        bank_consumed("ps1", s_act)
                        free_sG = []
                        v_sg = s_act.v
                        acol = ei * NFC * 1024 + fc * 1024
                        if e < NEXP:
                            wt("vector", s_ld, v_wrep[ei])
                            wait_list("vector", free_uW)
                            wt("vector", s_pe, v_gu)
                            op("vector", lambda ei=ei, wsl=wsl: E["vector"].tensor_mul(
                                uW[:, 0:512], ps[2][:],
                                wrep[:, wsl * 2048 + ei * 1024:
                                     wsl * 2048 + ei * 1024 + 512]))
                            op("vector", lambda ei=ei, wsl=wsl: E["vector"].tensor_mul(
                                uW[:, 512:1024], ps[3][:],
                                wrep[:, wsl * 2048 + ei * 1024 + 512:
                                     wsl * 2048 + (ei + 1) * 1024]),
                                incs=[(s_dve, 1)])
                            bank_consumed("ps2", s_dve)
                            bank_consumed("ps3", s_dve)
                            free_uW = []
                            wt("vector", s_act, v_sg)
                            wait_list("vector", free_act[aslot])
                            op("vector", lambda aslot=aslot, acol=acol:
                               E["vector"].tensor_mul(
                                actv[aslot][:, acol:acol + 1024], sG[:], uW[:]),
                                incs=[(s_dve, 1)])
                            free_uW = [(s_dve, s_dve.v)]
                        else:
                            wt("vector", s_act, v_sg)
                            wait_list("vector", free_act[aslot])
                            wt("vector", s_pe, v_gu)
                            op("vector", lambda aslot=aslot, acol=acol:
                               E["vector"].tensor_mul(
                                actv[aslot][:, acol:acol + 512],
                                sG[:, 0:512], ps[2][:]))
                            op("vector", lambda aslot=aslot, acol=acol:
                               E["vector"].tensor_mul(
                                actv[aslot][:, acol + 512:acol + 1024],
                                sG[:, 512:1024], ps[3][:]), incs=[(s_dve, 1)])
                            bank_consumed("ps2", s_dve)
                            bank_consumed("ps3", s_dve)
                        free_sG = [(s_dve, s_dve.v)]
                        free_act[aslot] = []
                    if e < NEXP and ei == 1:
                        free_wrep[wsl] = [(s_dve, s_dve.v)]
                v_act_grp = s_dve.v

                for dg in range(8):
                    dslot = dg % 2
                    v_dn = [0] * 12
                    wait_list("sync", free_dns[dslot])
                    for ei in range(2):
                        for fkc in range(NFC):
                            idx = ei * NFC + fkc
                            v_dn[idx] = dn_dma(dslot, idx, es[ei], fkc, dg)
                    free_dns[dslot] = []
                    for dcl in range(2):
                        dc = dg * 2 + dcl
                        for sub in range(NSB):
                            b = f"ps{4 + ((dcl * NSB + sub) % 2)}"
                            pst = ps[4 + ((dcl * NSB + sub) % 2)]
                            pe_claim(b)
                            if dcl == 0 and sub == 0:
                                wt("tensor", s_dve, v_act_grp)
                                wt("tensor", s_ld, v_dn[11])
                            nmm = 0
                            for ei in range(2):
                                for fkc in range(NFC):
                                    idx = ei * NFC + fkc
                                    col = dslot * 12 * 256 + idx * 256 + dcl * 128
                                    lhsT = bhalf(dns[:], col, 128)
                                    rhs_lo = ei * NFC * 1024 + fkc * 1024 + sub * 512
                                    nmm += 1
                                    op("tensor",
                                       lambda pst=pst, lhsT=lhsT, rhs_lo=rhs_lo,
                                       aslot=aslot,
                                       first=(nmm == 1), last=(nmm == 12):
                                       E["tensor"].matmul(
                                        pst[:], lhsT,
                                        actv[aslot][:, rhs_lo:rhs_lo + 512],
                                        start=first, stop=last,
                                        skip_group_check=True),
                                        incs=([(s_pe, 1)] if nmm == 12 else []))
                            v_dnmm = s_pe.v
                            wt("vector", s_pe, v_dnmm)
                            rcol = dc * 2048 + h * 1024 + sub * 512
                            if grp == 0:
                                op("vector", lambda rcol=rcol, pst=pst:
                                   E["vector"].tensor_copy(
                                    out=routv[:, rcol:rcol + 512],
                                    in_=pst[:]), incs=[(s_dve, 1)])
                            else:
                                op("vector", lambda rcol=rcol, pst=pst:
                                   E["vector"].tensor_tensor(
                                    out=routv[:, rcol:rcol + 512],
                                    in0=pst[:],
                                    in1=routv[:, rcol:rcol + 512],
                                    op=ALU.add), incs=[(s_dve, 1)])
                            bank_consumed(b, s_dve)
                    free_dns[dslot] = [(s_pe, s_pe.v)]
                free_act[aslot] = [(s_pe, s_pe.v)]
            free_zbh = [(s_pe, v_zbh_lastmm)]
        v_F_end = [s_pe.v, s_dve.v, s_act.v, s_st.v]

    ctxF.close()
    ctxF2 = ExitStack()
    ostg = ctxF2.enter_context(nc.sbuf_tensor("sb_ostg", [128, 4096], F32))
    if KPH >= 9:
        v_routed = v_F_end[1]
        free_xin2 = list(free_xin)
        free_ostg = [[], []]
        free_cvt = []
        for tt in range(T // 128):
            osl = tt % 2
            wait_list("sync", free_xin2)
            wt("sync", s_st, v_x2_spill[tt])
            op("sync", lambda tt=tt: E["sync"].dma_start(
                out=xin[:], in_=x2_d[tt * 128:(tt + 1) * 128, :]),
                incs=[(s_ld, 16)])
            v_x2r = s_ld.v
            for dc in range(NDC):
                wait_list("vector", free_cvt)
                if tt == 0 and dc == 0:
                    wt("vector", s_dve, v_routed)
                op("vector", lambda dc=dc, tt=tt: E["vector"].tensor_copy(
                    out=cvt[:],
                    in_=routv[:, dc * 2048 + tt * 128:
                              dc * 2048 + (tt + 1) * 128]),
                    incs=[(s_dve, 1)])
                v_cvt = s_dve.v
                b = f"pt{dc % 2}"
                wt("tensor", s_dve, v_cvt)
                pe_claim(b)
                op("tensor", lambda dc=dc: E["tensor"].transpose(
                    pt[dc % 2][:], cvt[:], iden[:]), incs=[(s_pe, 1)])
                free_cvt = [(s_pe, s_pe.v)]
                wt("vector", s_pe, s_pe.v)
                if dc == 0:
                    wt("vector", s_ld, v_x2r)
                    wait_list("vector", free_ostg[osl])
                op("vector", lambda dc=dc, osl=osl: E["vector"].tensor_add(
                    ostg[:, osl * 2048 + dc * 128: osl * 2048 + (dc + 1) * 128],
                    pt[dc % 2][:], xin[:, dc * 128:(dc + 1) * 128]),
                    incs=[(s_dve, 1)])
                bank_consumed(b, s_dve)
            free_xin2 = [(s_dve, s_dve.v)]
            free_ostg[osl] = []
            wt("sync", s_dve, s_dve.v)
            op("sync", lambda tt=tt, osl=osl: E["sync"].dma_start(
                out=out_h[tt * 128:(tt + 1) * 128, :],
                in_=ostg[:, osl * 2048:(osl + 1) * 2048]), incs=[(s_st, 16)])
            free_ostg[osl] = [(s_st, s_st.v)]

    wt("sync", s_st, s_st.v)
    wt("sync", s_ld, s_ld.v)

    finals = {s: s.v for s in ALLSEMS}
    for sem, val in WAITS:
        assert val <= finals[sem], (sem.name, val, finals[sem])

    with nc.Block() as block:
        block.sync(lambda eng: [f() for f in EM["sync"]])
        block.tensor(lambda eng: [f() for f in EM["tensor"]])
        block.vector(lambda eng: [f() for f in EM["vector"]])
        block.scalar(lambda eng: [f() for f in EM["scalar"]])
        block.gpsimd(lambda eng: [f() for f in EM["gpsimd"]])
    ctxF2.close()
    ctxR.close()
    ctxP.close()
    return nc


_NC_CACHE = []


def kernel(L0_attn_x, L0_attn_conv_w, L0_attn_conv_b, L0_attn_xproj_w,
           L0_attn_dtproj_w, L0_attn_dtproj_b, L0_attn_A_log, L0_attn_D_param,
           L0_ffn_gate_w, L0_ffn_wg, L0_ffn_wu, L0_ffn_wd, L0_ffn_sg,
           L0_ffn_su, L0_ffn_sd):
    f32 = lambda a: np.ascontiguousarray(np.asarray(a), dtype=np.float32)
    x = f32(L0_attn_x)
    base = dict(
        conv_w=f32(L0_attn_conv_w).reshape(D, 4),
        conv_b=f32(L0_attn_conv_b),
        xproj_w=f32(L0_attn_xproj_w),
        dtproj_w=f32(L0_attn_dtproj_w),
        dtproj_b=f32(L0_attn_dtproj_b),
        A_log=f32(L0_attn_A_log),
        D_param=f32(L0_attn_D_param),
        gate_w=f32(L0_ffn_gate_w),
        wg=f32(L0_ffn_wg), wu=f32(L0_ffn_wu), wd=f32(L0_ffn_wd),
        sg=f32(L0_ffn_sg), su=f32(L0_ffn_su), sd=f32(L0_ffn_sd),
        iden=np.eye(128, dtype=np.float32),
    )
    if not _NC_CACHE:
        _NC_CACHE.append(build_nc())
    nc = _NC_CACHE[0]
    in_maps = [dict(base, x=np.ascontiguousarray(x[i])) for i in range(8)]
    res = run_bass_kernel_spmd(nc, in_maps, core_ids=list(range(8)))
    out = np.stack([np.asarray(res.results[i]["out"]) for i in range(8)],
                   axis=0)
    return out
